# version 01
# baseline (speedup 1.0000x reference)
import numpy as np
import ml_dtypes
from contextlib import ExitStack

import concourse.bass as bass
import concourse.mybir as mybir
from concourse.bass_utils import run_bass_kernel_spmd

F32 = mybir.dt.float32
BF16 = mybir.dt.bfloat16
I32 = mybir.dt.int32
AF = mybir.ActivationFunctionType
ALU = mybir.AluOpType
AX = mybir.AxisListType
BF = ml_dtypes.bfloat16

B, S, D = 2, 8192, 2048
H, DH, G = 16, 128, 4
KV0 = G * DH
DFF = 4 * D
HID = 5632
CHT = 1024
R = 2048
NT = R // 128
EPS = 1e-5
SCL = float(1.0 / np.sqrt(np.float32(DH)))
NUNIT = 9
TOPK = 1024


def core_chunks(c):
    j = c % 4
    return (j, 7 - j)


def unit_list(c):
    ja, jb = core_chunks(c)
    units = [(0, ja), (1, jb)]
    units += [(0, k) for k in range(ja)]
    units += [(1, k) for k in range(jb)]
    return units



ENGINES = ("sync", "gpsimd", "scalar", "vector", "tensor")


def _key(x):
    if isinstance(x, str):
        return x
    if isinstance(x, tuple):
        return x
    t = getattr(x, "tensor", None)
    if t is not None:
        return t.name
    return x.name


class Rec:

    def __init__(self):
        self.nc = bass.Bass("TRN2", target_bir_lowering=False, debug=False)
        self.es = ExitStack()
        self.steps = []
        self.last_writer = {}
        self.readers = {}
        self.groups = {}
        self.alloc_order = []
        self.barrier_idx = None

    def sb(self, name, shape, dt, group=None):
        cm = self.nc.sbuf_tensor(name, shape, dt)
        h = cm.__enter__()
        self.groups.setdefault(group, []).append(cm)
        self.alloc_order.append(cm)
        return h

    def free_group(self, group):
        for cm in reversed(self.groups.pop(group, [])):
            cm.__exit__(None, None, None)
            self.alloc_order.remove(cm)

    def barrier(self):
        sinks = set(range(len(self.steps)))
        for st in self.steps:
            sinks -= st["deps"]
        idx = len(self.steps)
        self.steps.append(dict(engine="sync", fn=lambda e: e.nop(), deps=set(sinks),
                               dma=False, rk=[], wk=[], tag="barrier", idx=idx))
        self.barrier_idx = idx
        return idx

    def psum(self, name, shape, dt=F32, group=None):
        cm = self.nc.psum_tensor(name, shape, dt)
        h = cm.__enter__()
        self.groups.setdefault(group, []).append(cm)
        self.alloc_order.append(cm)
        return h

    def dram(self, name, shape, dt, kind="Internal", addr_space="Local"):
        return self.nc.dram_tensor(name, shape, dt, kind=kind, addr_space=addr_space).ap()

    def step(self, engine, fn, r=(), w=(), dma=False, tag="", fence=False):
        idx = len(self.steps)
        deps = set()
        rk = [_key(x) for x in r]
        wk = [_key(x) for x in w]
        for k in rk:
            if k in self.last_writer:
                deps.add(self.last_writer[k])
        for k in wk:
            if k in self.last_writer:
                deps.add(self.last_writer[k])
            for rd in self.readers.get(k, ()):
                deps.add(rd)
        if self.barrier_idx is not None:
            deps.add(self.barrier_idx)
        self.steps.append(dict(engine=engine, fn=fn, deps=deps, dma=dma,
                               rk=rk, wk=wk, tag=tag, idx=idx, fence=fence))
        for k in wk:
            self.last_writer[k] = idx
            self.readers[k] = []
        for k in rk:
            self.readers.setdefault(k, []).append(idx)
        return idx

    def t(self, fn, r=(), w=(), tag=""):
        return self.step("tensor", fn, r, w, tag=tag)

    def v(self, fn, r=(), w=(), tag="", fence=False):
        return self.step("vector", fn, r, w, tag=tag, fence=fence)

    def a(self, fn, r=(), w=(), tag=""):
        return self.step("scalar", fn, r, w, tag=tag)

    def g(self, fn, r=(), w=(), tag=""):
        return self.step("gpsimd", fn, r, w, tag=tag)

    def s(self, fn, r=(), w=(), tag=""):
        return self.step("sync", fn, r, w, tag=tag)

    def dma(self, queue, out, in_, r=(), w=(), tag="", **kw):
        return self.step(queue, lambda e: e.dma_start(out=out, in_=in_, **kw),
                         r=list(r) + [in_], w=list(w) + [out], dma=True, tag=tag)

    def emit(self):
        nsteps = len(self.steps)
        awaited = [False] * nsteps
        for st in self.steps:
            for d in st["deps"]:
                dst = self.steps[d]
                if (not dst["dma"]) and dst["engine"] == st["engine"] == "tensor":
                    continue
                awaited[d] = True
        sems = {}
        for en in ENGINES:
            sems[en] = self.es.enter_context(self.nc.semaphore(f"p_{en}"))
            sems[en + "_d"] = self.es.enter_context(self.nc.semaphore(f"d_{en}"))
        incno = [0] * nsteps
        ctr = {k: 0 for k in sems}
        for i, st in enumerate(self.steps):
            if awaited[i]:
                key = st["engine"] + ("_d" if st["dma"] else "")
                ctr[key] += 16 if st["dma"] else 1
                incno[i] = ctr[key]
        per_engine = {en: [] for en in ENGINES}
        for st in self.steps:
            per_engine[st["engine"]].append(st)

        def make_body(en):
            def body(eng):
                floor = {}
                for st in per_engine[en]:
                    wants = {}
                    for d in sorted(st["deps"]):
                        dst = self.steps[d]
                        if (not dst["dma"]) and dst["engine"] == en == "tensor":
                            continue
                        key = dst["engine"] + ("_d" if dst["dma"] else "")
                        assert awaited[d] and incno[d] > 0
                        wants[key] = max(wants.get(key, 0), incno[d])
                    for key, val in sorted(wants.items()):
                        if floor.get(key, 0) >= val:
                            continue
                        eng.wait_ge(sems[key], val)
                        floor[key] = val
                    inst = st["fn"](eng)
                    if awaited[st["idx"]]:
                        key = en + ("_d" if st["dma"] else "")
                        inst.then_inc(sems[key], 16 if st["dma"] else 1)
            return body

        with self.nc.Block() as block:
            for en in ENGINES:
                if per_engine[en]:
                    getattr(block, en)(make_body(en))
        return self.nc

    def finish(self, final_engine="sync"):
        sinks = set(range(len(self.steps)))
        for st in self.steps:
            sinks -= st["deps"]
        sinks = sorted(sinks)

        def fn(e):
            return e.nop()
        idx = len(self.steps)
        self.steps.append(dict(engine=final_engine, fn=fn, deps=set(sinks),
                               dma=False, rk=[], wk=[], tag="final", idx=idx))
        nc = self.emit()
        for cm in reversed(self.alloc_order):
            cm.__exit__(None, None, None)
        self.groups.clear()
        self.alloc_order = []
        self.es.close()
        return nc



def build_launch_a(upto=99):
    rc = Rec()
    nc = rc.nc
    xo = rc.dram("xo", [R, D], F32, kind="ExternalInput")
    wq = rc.dram("wq", [D, D], F32, kind="ExternalInput")
    wk = rc.dram("wk", [D, KV0], F32, kind="ExternalInput")
    wv = rc.dram("wv", [D, KV0], F32, kind="ExternalInput")
    wr = rc.dram("wr", [1, D], F32, kind="ExternalInput")
    brt = rc.dram("brt", [1, 1], F32, kind="ExternalInput")
    xn_tm = rc.dram("xn_tm", [R, D], BF16, kind="ExternalOutput")
    q0 = rc.dram("q0", [D, R], BF16, kind="ExternalOutput")
    k0 = rc.dram("k0", [KV0, R], BF16, kind="ExternalOutput")
    v0 = rc.dram("v0", [R, KV0], BF16, kind="ExternalOutput")
    lg = rc.dram("lg", [1, R], F32, kind="ExternalOutput")

    x_sb = [rc.sb(f"x_sb{i}", [128, D], F32) for i in range(2)]
    sq = rc.sb("sq", [128, D], F32)
    s1 = rc.sb("s1", [128, 1], F32)
    mmean = rc.sb("mmean", [128, 1], F32)
    var = rc.sb("var", [128, 1], F32)
    rstd = rc.sb("rstd", [128, 1], F32)
    xn_sb = [rc.sb(f"xn_sb{i}", [128, D], BF16) for i in range(2)]
    xnf = rc.sb("xnf", [128, D], F32)
    wrb = rc.sb("wrb", [128, D], F32)
    brb = rc.sb("brb", [128, 1], F32)
    lgcol = rc.sb("lgcol", [128, NT], F32)
    wr1 = rc.sb("wr1", [1, D], F32)
    br1 = rc.sb("br1", [1, 1], F32)
    ones_r = rc.sb("ones_r", [1, 128], F32)
    eps_sb = rc.sb("eps_sb", [128, 1], F32)
    xfm = [rc.sb(f"xfm{i}", [128, R], BF16) for i in range(16)]
    ostg = [rc.sb(f"ostg{i}", [128, R], BF16) for i in range(2)]
    bc_ps = rc.psum("bc_ps", [128, D])
    br_ps = rc.psum("br_ps", [128, 1])
    mm_ps = [rc.psum(f"mm_ps{i}", [128, 512]) for i in range(2)]

    rc.dma("sync", wr1[:], wr)
    rc.dma("sync", br1[:], brt)
    rc.v(lambda e: e.memset(ones_r[:], 1.0), w=[ones_r])
    rc.v(lambda e: e.memset(eps_sb[:], float(EPS)), w=[eps_sb])
    for i4 in range(4):
        rc.t(lambda e, i4=i4: e.matmul(bc_ps[:, i4 * 512:(i4 + 1) * 512], ones_r[:],
                                       wr1[:, i4 * 512:(i4 + 1) * 512], start=True, stop=True),
             r=[ones_r, wr1], w=[bc_ps])
    rc.t(lambda e: e.matmul(br_ps[:], ones_r[:], br1[:], start=True, stop=True),
         r=[ones_r, br1], w=[br_ps])
    rc.v(lambda e: e.tensor_copy(out=wrb[:], in_=bc_ps[:]), r=[bc_ps], w=[wrb])
    rc.v(lambda e: e.tensor_copy(out=brb[:], in_=br_ps[:]), r=[br_ps], w=[brb])

    for rt in range(NT):
        xt = x_sb[rt % 2]
        xn = xn_sb[rt % 2]
        rc.dma("sync", xt[:], xo[rt * 128:(rt + 1) * 128, :], tag=f"ldx{rt}")
        rc.a(lambda e, xt=xt: e.activation(out=sq[:], in_=xt[:], func=AF.Square),
             r=[xt], w=[sq])
        rc.v(lambda e, xt=xt: e.reduce_sum(out=s1[:], in_=xt[:], axis=AX.X), r=[xt], w=[s1])
        rc.v(lambda e: e.tensor_scalar_mul(out=mmean[:], in0=s1[:], scalar1=1.0 / D),
             r=[s1], w=[mmean])
        rc.v(lambda e: e.reduce_sum(out=s1[:], in_=sq[:], axis=AX.X), r=[sq], w=[s1])
        rc.v(lambda e: e.tensor_mul(out=var[:], in0=mmean[:], in1=mmean[:]), r=[mmean], w=[var])
        rc.v(lambda e: e.tensor_scalar(out=var[:], in0=s1[:], scalar1=1.0 / D,
                                       scalar2=var[:, 0:1], op0=ALU.mult, op1=ALU.subtract),
             r=[s1, var], w=[var])
        rc.a(lambda e: e.activation(out=var[:], in_=var[:], func=AF.Ln, bias=eps_sb[:, 0:1]),
             r=[var, eps_sb], w=[var])
        rc.a(lambda e: e.activation(out=rstd[:], in_=var[:], func=AF.Exp, scale=-0.5),
             r=[var], w=[rstd])
        rc.v(lambda e, xt=xt, xn=xn: e.tensor_scalar(
            out=xn[:], in0=xt[:], scalar1=mmean[:, 0:1], scalar2=rstd[:, 0:1],
            op0=ALU.subtract, op1=ALU.mult), r=[xt, mmean, rstd], w=[xn])
        rc.v(lambda e, xt=xt: e.tensor_scalar(
            out=xnf[:], in0=xt[:], scalar1=mmean[:, 0:1], scalar2=rstd[:, 0:1],
            op0=ALU.subtract, op1=ALU.mult), r=[xt, mmean, rstd], w=[xnf])
        rc.v(lambda e: e.tensor_mul(out=xnf[:], in0=xnf[:], in1=wrb[:]), r=[xnf, wrb], w=[xnf])
        rc.v(lambda e, rt=rt: e.reduce_sum(out=lgcol[:, rt:rt + 1], in_=xnf[:], axis=AX.X),
             r=[xnf], w=[("lgcol", rt)])
        rc.v(lambda e, rt=rt: e.tensor_scalar_add(
            out=lgcol[:, rt:rt + 1], in0=lgcol[:, rt:rt + 1], scalar1=brb[:, 0:1]),
            r=[("lgcol", rt), brb], w=[("lgcol", rt)])
        rc.dma("gpsimd", xn_tm[rt * 128:(rt + 1) * 128, :], xn[:], tag=f"stxn{rt}")
    rc.dma("gpsimd", lg.rearrange("one (rt p) -> p (one rt)", p=128), lgcol[:],
           r=[("lgcol", rt) for rt in range(NT)], w=["lg_done"], allow_slow_non_contiguous=True)

    if upto <= 1:
        return rc.finish()
    for dc in range(16):
        rc.step("scalar", lambda e, dc=dc: e.dma_start(
            out=xfm[dc][:], in_=xn_tm[:, dc * 128:(dc + 1) * 128], transpose=True),
            r=[xn_tm, "lg_done"], w=[xfm[dc]], dma=True, tag=f"tr{dc}")
    rc.s(lambda e: e.nop(), r=[xfm[dc] for dc in range(16)], w=["xbar_quiet"], tag="xbarq")

    def proj_fm(wsrc, wname, n_out_chunks, out_dram):
        for oc in range(n_out_chunks):
            pass
    if upto <= 2:
        return rc.finish()
    wq_sb = [rc.sb(f"wq_sb{i}", [128, D], BF16) for i in range(16)]
    for kc in range(16):
        rc.dma("gpsimd", wq_sb[kc][:], wq[kc * 128:(kc + 1) * 128, :], r=["xbar_quiet"], tag=f"ldwq{kc}")
    for wps in range(2):
        rc.t(lambda e, wps=wps: e.matmul(mm_ps[wps][:], wq_sb[0][:, 0:128],
                                         xfm[0][:, 0:512], start=True, stop=True),
             r=[wq_sb[0], xfm[0]], w=[mm_ps[wps]], tag="warm")
        rc.v(lambda e, wps=wps: e.tensor_copy(out=ostg[0][:, 0:512], in_=mm_ps[wps][:]),
             r=[mm_ps[wps]], w=[("ostg0", 0)], tag="warmev")
    for oc in range(16):
        stg = ostg[oc % 2]
        for tg in range(4):
            pp = mm_ps[tg % 2]
            for kc in range(16):
                rc.t(lambda e, oc=oc, tg=tg, kc=kc, pp=pp: e.matmul(
                    pp[:], wq_sb[kc][:, oc * 128:(oc + 1) * 128],
                    xfm[kc][:, tg * 512:(tg + 1) * 512],
                    start=(kc == 0), stop=(kc == 15)),
                    r=[wq_sb[kc], xfm[kc]], w=[pp], tag=f"q0mm{oc}")
            rc.v(lambda e, tg=tg, pp=pp, stg=stg: e.tensor_copy(
                out=stg[:, tg * 512:(tg + 1) * 512], in_=pp[:]),
                r=[pp], w=[(stg.name, tg)])
        rc.dma("gpsimd", q0[oc * 128:(oc + 1) * 128, :], stg[:],
               r=[(stg.name, tg) for tg in range(4)], tag=f"stq0{oc}")
    if upto <= 3:
        return rc.finish()
    wk_sb = [wq_sb[i][:, 0:KV0] for i in range(16)]
    wv_sb = [wq_sb[i][:, KV0:2 * KV0] for i in range(16)]
    for kc in range(16):
        rc.dma("gpsimd", wk_sb[kc][:], wk[kc * 128:(kc + 1) * 128, :], tag=f"ldwk{kc}")
        rc.dma("gpsimd", wv_sb[kc][:], wv[kc * 128:(kc + 1) * 128, :], tag=f"ldwv{kc}")
    for oc in range(4):
        stg = ostg[oc % 2]
        for tg in range(4):
            pp = mm_ps[tg % 2]
            for kc in range(16):
                rc.t(lambda e, oc=oc, tg=tg, kc=kc, pp=pp: e.matmul(
                    pp[:], wk_sb[kc][:, oc * 128:(oc + 1) * 128],
                    xfm[kc][:, tg * 512:(tg + 1) * 512],
                    start=(kc == 0), stop=(kc == 15)),
                    r=[wq_sb[kc], xfm[kc]], w=[pp], tag=f"k0mm{oc}")
            rc.v(lambda e, tg=tg, pp=pp, stg=stg: e.tensor_copy(
                out=stg[:, tg * 512:(tg + 1) * 512], in_=pp[:]),
                r=[pp], w=[(stg.name, tg)])
        rc.dma("gpsimd", k0[oc * 128:(oc + 1) * 128, :], stg[:],
               r=[(stg.name, tg) for tg in range(4)], tag=f"stk0{oc}")
    for tt in range(NT):
        stg = ostg[tt % 2]
        pp = mm_ps[tt % 2]
        for kc in range(16):
            rc.t(lambda e, tt=tt, kc=kc, pp=pp: e.matmul(
                pp[:], xfm[kc][:, tt * 128:(tt + 1) * 128], wv_sb[kc],
                start=(kc == 0), stop=(kc == 15)),
                r=[wq_sb[kc], xfm[kc]], w=[pp], tag=f"v0mm{tt}")
        rc.v(lambda e, pp=pp, stg=stg: e.tensor_copy(out=stg[:, 0:KV0], in_=pp[:]),
             r=[pp], w=[(stg.name, 0)])
        rc.dma("gpsimd", v0[tt * 128:(tt + 1) * 128, :], stg[:, 0:KV0],
               r=[(stg.name, 0)], tag=f"stv0{tt}")

    return rc.finish()



def build_launch_b(upto=99):
    rc = Rec()
    nc = rc.nc
    xo = rc.dram("xo", [R, D], F32, kind="ExternalInput")
    xn_tm = rc.dram("xn_tm", [R, D], BF16, kind="ExternalInput")
    q0s = rc.dram("q0s", [NUNIT, D, CHT], BF16, kind="ExternalInput")
    k0s = rc.dram("k0s", [NUNIT, KV0, CHT], BF16, kind="ExternalInput")
    v0s = rc.dram("v0s", [NUNIT, CHT, KV0], BF16, kind="ExternalInput")
    lgb = rc.dram("lgb", [1, S], F32, kind="ExternalInput")
    lgo = rc.dram("lgo", [1, R], F32, kind="ExternalInput")
    masks = rc.dram("masks", [8, 128, 512], BF16, kind="ExternalInput")
    cmbw = rc.dram("cmbw", [128, 2 * NUNIT], F32, kind="ExternalInput")
    idf = rc.dram("idf", [128, 128], F32, kind="ExternalInput")
    wo0 = rc.dram("wo0", [D, D], F32, kind="ExternalInput")
    bo = rc.dram("bo", [1, D], F32, kind="ExternalInput")
    w1 = rc.dram("w1", [D, DFF], F32, kind="ExternalInput")
    b1 = rc.dram("b1", [1, DFF], F32, kind="ExternalInput")
    w2 = rc.dram("w2", [DFF, D], F32, kind="ExternalInput")
    b2 = rc.dram("b2", [1, D], F32, kind="ExternalInput")
    wqkvT = rc.dram("wqkvT", [D, 3 * D], F32, kind="ExternalInput")
    cosq = rc.dram("cosq", [128, R], F32, kind="ExternalInput")
    sinq = rc.dram("sinq", [128, R], F32, kind="ExternalInput")
    ropeR = rc.dram("ropeR", [128, 128], F32, kind="ExternalInput")

    x1o = rc.dram("x1o", [R, D], F32, kind="ExternalOutput")
    xn1_tm = rc.dram("xn1_tm", [R, D], BF16, kind="ExternalOutput")
    q1 = rc.dram("q1", [D, R], BF16, kind="ExternalOutput")
    k1 = rc.dram("k1", [D, R], BF16, kind="ExternalOutput")
    v1 = rc.dram("v1", [R, D], BF16, kind="ExternalOutput")
    dbgB = rc.dram("dbgB", [128, 24], F32, kind="ExternalOutput")

    slots = rc.dram("slots", [NUNIT, D, CHT], F32)
    slotsZ = rc.dram("slotsZ", [NUNIT, 1, H * CHT], F32)

    w_own = rc.sb("w_own", [128, NT], F32)
    lgo_sb = rc.sb("lgo_sb", [128, NT], F32)
    dbg_sb = rc.sb("dbg_sb", [128, 24], F32)
    lgs = rc.sb("lgs", [128, 64], F32, group="b0")
    t128 = rc.sb("t128", [128, 128], F32, group="b0")
    ids = rc.sb("ids", [128, 128], F32, group="b0")
    onesr = rc.sb("onesr", [1, 128], F32, group="b0")
    ones128 = rc.sb("ones128", [128, 1], F32, group="b0")
    lo = rc.sb("lo", [128, 1], F32, group="b0")
    hi = rc.sb("hi", [128, 1], F32, group="b0")
    tb = rc.sb("tb", [128, 1], F32, group="b0")
    cm = rc.sb("cm", [128, 64], F32, group="b0")
    rowc = rc.sb("rowc", [128, 1], F32, group="b0")
    cntb = rc.sb("cntb", [128, 1], F32, group="b0")
    geb = rc.sb("geb", [128, 1], F32, group="b0")
    tmp1 = rc.sb("tmp1", [128, 1], F32, group="b0")
    negM = rc.sb("negM", [128, 1], F32, group="b0")
    rzb = rc.sb("rzb", [128, 1], F32, group="b0")
    taub = rc.sb("taub", [128, 1], F32, group="b0")
    p1x = rc.psum("p1x", [1, 128], group="b0ps")
    pbc = rc.psum("pbc", [128, 128], group="b0ps")

    rc.dma("sync", lgs[:], lgb.rearrange("one (p f) -> (one p) f", p=128))
    rc.dma("sync", ids[:], idf)
    rc.dma("sync", lgo_sb[:], lgo.rearrange("one (rt p) -> p (one rt)", p=128),
           allow_slow_non_contiguous=True)
    rc.v(lambda e: e.memset(onesr[:], 1.0), w=[onesr])
    rc.v(lambda e: e.memset(ones128[:], 1.0), w=[ones128])

    def xpart_reduce(src_col, dst, op):
        rc.v(lambda e: e.tensor_copy(out=t128[:, 0:1], in_=src_col), r=[src_col], w=[t128])
        rc.t(lambda e: e.transpose(pbc[:], t128[:], ids[:]), r=[t128, ids], w=[pbc])
        rc.v(lambda e: e.tensor_reduce(out=t128[0:1, 1:2], in_=pbc[0:1, 0:1].broadcast_to((1, 128)) if False else pbc[0:1, :], axis=AX.X, op=op),
             r=[pbc], w=[t128])
        rc.t(lambda e: e.matmul(pbc[:, 0:1], onesr[:], t128[0:1, 1:2], start=True, stop=True),
             r=[onesr, t128], w=[pbc])
        rc.v(lambda e: e.tensor_copy(out=dst, in_=pbc[:, 0:1]), r=[pbc], w=[dst])

    rc.v(lambda e: e.reduce_max(out=rowc[:], in_=lgs[:], axis=AX.X), r=[lgs], w=[rowc])
    xpart_reduce(rowc[:, 0:1], hi[:, 0:1], ALU.max)
    rc.v(lambda e: e.tensor_scalar_mul(out=cm[:], in0=lgs[:], scalar1=-1.0), r=[lgs], w=[cm])
    rc.v(lambda e: e.reduce_max(out=rowc[:], in_=cm[:], axis=AX.X), r=[cm], w=[rowc])
    xpart_reduce(rowc[:, 0:1], lo[:, 0:1], ALU.max)
    rc.v(lambda e: e.tensor_scalar_mul(out=negM[:], in0=hi[:], scalar1=-1.0), r=[hi], w=[negM])
    rc.v(lambda e: e.tensor_scalar(out=lo[:], in0=lo[:], scalar1=-1.0, scalar2=1.0,
                                   op0=ALU.mult, op1=ALU.subtract), r=[lo], w=[lo])
    rc.v(lambda e: e.tensor_scalar_add(out=hi[:], in0=hi[:], scalar1=1.0), r=[hi], w=[hi])

    for it in range(40):
        rc.v(lambda e: e.tensor_scalar(out=tb[:], in0=lo[:], scalar1=hi[:, 0:1], scalar2=0.5,
                                       op0=ALU.add, op1=ALU.mult), r=[lo, hi], w=[tb])
        rc.v(lambda e: e.tensor_single_scalar(out=cm[:], in_=lgs[:], scalar=tb[:, 0:1],
                                              op=ALU.is_ge), r=[lgs, tb], w=[cm])
        rc.v(lambda e: e.reduce_sum(out=rowc[:], in_=cm[:], axis=AX.X), r=[cm], w=[rowc])
        rc.t(lambda e: e.matmul(p1x[:, 0:1], rowc[:], ones128[:], start=True, stop=True),
             r=[rowc, ones128], w=[p1x])
        rc.v(lambda e: e.tensor_copy(out=t128[0:1, 2:3], in_=p1x[:, 0:1]), r=[p1x], w=[t128])
        rc.t(lambda e: e.matmul(pbc[:, 0:1], onesr[:], t128[0:1, 2:3], start=True, stop=True),
             r=[onesr, t128], w=[pbc])
        rc.v(lambda e: e.tensor_copy(out=cntb[:], in_=pbc[:, 0:1]), r=[pbc], w=[cntb])
        rc.v(lambda e: e.tensor_single_scalar(out=geb[:], in_=cntb[:], scalar=float(TOPK),
                                              op=ALU.is_ge), r=[cntb], w=[geb])
        rc.v(lambda e: e.tensor_sub(out=tmp1[:], in0=tb[:], in1=lo[:]), r=[tb, lo], w=[tmp1])
        rc.v(lambda e: e.tensor_mul(out=tmp1[:], in0=tmp1[:], in1=geb[:]), r=[tmp1, geb], w=[tmp1])
        rc.v(lambda e: e.tensor_add(out=lo[:], in0=lo[:], in1=tmp1[:]), r=[lo, tmp1], w=[lo])
        rc.v(lambda e: e.tensor_sub(out=tmp1[:], in0=hi[:], in1=tb[:]), r=[hi, tb], w=[tmp1])
        rc.v(lambda e: e.tensor_scalar(out=geb[:], in0=geb[:], scalar1=-1.0, scalar2=1.0,
                                       op0=ALU.mult, op1=ALU.add), r=[geb], w=[geb])
        rc.v(lambda e: e.tensor_mul(out=tmp1[:], in0=tmp1[:], in1=geb[:]), r=[tmp1, geb], w=[tmp1])
        rc.v(lambda e: e.tensor_sub(out=hi[:], in0=hi[:], in1=tmp1[:]), r=[hi, tmp1], w=[hi])
    rc.v(lambda e: e.tensor_copy(out=taub[:], in_=lo[:]), r=[lo], w=[taub])
    rc.a(lambda e: e.activation(out=cm[:], in_=lgs[:], func=AF.Exp, bias=negM[:, 0:1]),
         r=[lgs, negM], w=[cm])
    rc.v(lambda e: e.tensor_single_scalar(out=t128[:, 3:4].broadcast_to((128, 1)) if False else lgs[:], in_=lgs[:], scalar=taub[:, 0:1], op=ALU.is_ge),
         r=[lgs, taub], w=[lgs], tag="selmask")
    rc.v(lambda e: e.tensor_mul(out=cm[:], in0=cm[:], in1=lgs[:]), r=[cm, lgs], w=[cm])
    rc.v(lambda e: e.reduce_sum(out=rowc[:], in_=cm[:], axis=AX.X), r=[cm], w=[rowc])
    rc.t(lambda e: e.matmul(p1x[:, 0:1], rowc[:], ones128[:], start=True, stop=True),
         r=[rowc, ones128], w=[p1x])
    rc.v(lambda e: e.tensor_copy(out=t128[0:1, 4:5], in_=p1x[:, 0:1]), r=[p1x], w=[t128])
    rc.t(lambda e: e.matmul(pbc[:, 0:1], onesr[:], t128[0:1, 4:5], start=True, stop=True),
         r=[onesr, t128], w=[pbc])
    rc.v(lambda e: e.tensor_copy(out=cntb[:], in_=pbc[:, 0:1]), r=[pbc], w=[cntb])
    rc.v(lambda e: e.reciprocal(out=rzb[:], in_=cntb[:]), r=[cntb], w=[rzb])
    rc.v(lambda e: e.tensor_single_scalar(out=w_own[:], in_=lgo_sb[:], scalar=taub[:, 0:1],
                                          op=ALU.is_ge), r=[lgo_sb, taub], w=[w_own])
    rc.a(lambda e: e.activation(out=lgo_sb[:], in_=lgo_sb[:], func=AF.Exp, bias=negM[:, 0:1]),
         r=[lgo_sb, negM], w=[lgo_sb])
    rc.v(lambda e: e.tensor_mul(out=w_own[:], in0=w_own[:], in1=lgo_sb[:]), r=[w_own, lgo_sb], w=[w_own])
    rc.v(lambda e: e.tensor_scalar_mul(out=w_own[:], in0=w_own[:], scalar1=rzb[:, 0:1]),
         r=[w_own, rzb], w=[w_own])
    rc.v(lambda e: e.tensor_copy(out=t128[:, 5:6], in_=taub[:]), r=[taub], w=[t128])
    rc.v(lambda e: e.tensor_copy(out=dbg_sb[:, 0:1], in_=taub[:]), r=[taub], w=[dbg_sb])
    rc.v(lambda e: e.tensor_copy(out=dbg_sb[:, 1:2], in_=negM[:]), r=[negM], w=[dbg_sb])
    rc.v(lambda e: e.tensor_copy(out=dbg_sb[:, 2:3], in_=cntb[:]), r=[cntb], w=[dbg_sb])
    rc.v(lambda e: e.tensor_copy(out=dbg_sb[:, 3:3 + NT], in_=w_own[:]), r=[w_own], w=[dbg_sb])
    rc.dma("gpsimd", dbgB, dbg_sb[:])

    if upto <= 0:
        return rc.finish()

    rc.barrier()
    rc.free_group("b0ps")
    rc.free_group("b0")

    qsb = [rc.sb(f"qsb{i}", [128, H * CHT], BF16, group="b1") for i in range(2)]
    ksb = [rc.sb(f"ksb{i}", [128, G * CHT], BF16, group="b1") for i in range(2)]
    vsb = [rc.sb(f"vsb{i}", [128, 8 * KV0], BF16, group="b1") for i in range(2)]
    msk = rc.sb("msk", [128, 8 * 512], BF16, group="b1")
    onesbf = rc.sb("onesbf", [128, 1], BF16, group="b1")
    psb = rc.sb("psb", [128, 512], BF16, group="b1")
    cst = rc.sb("cst", [128, 512], F32, group="b1")
    zst = rc.sb("zst", [1, H * CHT], F32, group="b1")
    psc = [rc.psum(f"psc{i}", [128, 512], group="b1ps") for i in range(2)]
    pctx = rc.psum("pctx", [128, 512], group="b1ps")
    pz = rc.psum("pz", [1, 512], group="b1ps")

    rc.dma("sync", msk[:].rearrange("p (m q) -> p m q", m=8), masks.rearrange("m p q -> p m q"))
    rc.v(lambda e: e.memset(onesbf[:], 1.0), w=[onesbf])

    for u in range(NUNIT):
        diag = u < 2
        qb, kb, vb = qsb[u % 2], ksb[u % 2], vsb[u % 2]
        rc.dma("sync", qb[:].rearrange("p (h q) -> p h q", h=H), q0s[u].rearrange("(h p) q -> p h q", p=128), tag=f"ldq{u}")
        rc.dma("sync", kb[:].rearrange("p (g k) -> p g k", g=G), k0s[u].rearrange("(g p) k -> p g k", p=128), tag=f"ldk{u}")
        rc.dma("sync", vb[:].rearrange("p (a v) -> p a v", a=8), v0s[u].rearrange("(a p) v -> p a v", p=128), tag=f"ldv{u}")
        sci = 0
        for h in range(H):
            g = h // 4
            for qg in range(2):
                kts = list(range(4)) if (diag and qg == 0) else list(range(8))
                nk = len(kts)
                for i, kt in enumerate(kts):
                    pp = psc[sci % 2]
                    sci += 1
                    rc.t(lambda e, kb=kb, qb=qb, g=g, kt=kt, h=h, qg=qg, pp=pp: e.matmul(
                        pp[:], kb[:, g * CHT + kt * 128: g * CHT + (kt + 1) * 128],
                        qb[:, h * CHT + qg * 512: h * CHT + (qg + 1) * 512],
                        start=True, stop=True), r=[kb, qb], w=[pp], tag=f"sc{u}")
                    rc.a(lambda e, pp=pp: e.activation(out=psb[:], in_=pp[:], func=AF.Exp,
                                                       scale=SCL), r=[pp], w=[psb], tag=f"exp{u}")
                    if diag and ((qg == 0 and kt < 4) or (qg == 1 and kt >= 4)):
                        mi = kt if qg == 0 else kt
                        rc.v(lambda e, mi=mi: e.tensor_mul(
                            out=psb[:], in0=psb[:], in1=msk[:, mi * 512:(mi + 1) * 512]),
                            r=[psb, msk], w=[psb], tag=f"mask{u}")
                    rc.t(lambda e, vb=vb, kt=kt, g=g, i=i, nk=nk: e.matmul(
                        pctx[:], vb[:, kt * KV0 + g * 128: kt * KV0 + (g + 1) * 128], psb[:],
                        start=(i == 0), stop=(i == nk - 1)), r=[vb, psb], w=[pctx], tag=f"cx{u}")
                    rc.t(lambda e, i=i, nk=nk: e.matmul(
                        pz[:], onesbf[:], psb[:], start=(i == 0), stop=(i == nk - 1)),
                        r=[onesbf, psb], w=[pz], tag=f"z{u}")
                rc.v(lambda e: e.tensor_copy(out=cst[:], in_=pctx[:]), r=[pctx], w=[cst])
                rc.v(lambda e, h=h, qg=qg: e.tensor_copy(
                    out=zst[0:1, h * CHT + qg * 512: h * CHT + (qg + 1) * 512], in_=pz[:]),
                    r=[pz], w=[("zst", h, qg)])
                rc.dma("gpsimd", slots[u][h * 128:(h + 1) * 128, qg * 512:(qg + 1) * 512],
                       cst[:], tag=f"stc{u}")
        rc.dma("gpsimd", slotsZ[u][:, :], zst[:],
               r=[("zst", h, qg) for h in range(H) for qg in range(2)], tag=f"stz{u}")

    if upto <= 1:
        return rc.finish()

    rc.barrier()
    rc.free_group("b1ps")
    rc.free_group("b1")
    return _launch_b_tail(rc, locals(), upto)


def _launch_b_tail(rc, env, upto):
    slots, slotsZ = env["slots"], env["slotsZ"]
    cmbw, idf = env["cmbw"], env["idf"]
    xo, xn_tm, w_own = env["xo"], env["xn_tm"], env["w_own"]
    wo0, bo, w1, b1, w2, b2 = env["wo0"], env["bo"], env["w1"], env["b1"], env["w2"], env["b2"]
    wqkvT, cosq, sinq, ropeR = env["wqkvT"], env["cosq"], env["sinq"], env["ropeR"]
    x1o, xn1_tm, q1, k1, v1 = env["x1o"], env["xn1_tm"], env["q1"], env["k1"], env["v1"]
    hdram = rc.dram("hdram", [DFF, R], BF16)

    ctxf = [rc.sb(f"ctxf{i}", [128, R], BF16, group="ctxfg") for i in range(16)]
    cw = rc.sb("cw", [128, 2 * NUNIT], F32, group="b2")
    stile = [rc.sb(f"stile{i}", [128, CHT], F32, group="b2") for i in range(2)]
    acc = [rc.sb(f"acc{i}", [128, CHT], F32, group="b2") for i in range(2)]
    zsb = [rc.sb(f"zsb{i}", [H, CHT], F32, group="b2") for i in range(2)]
    zac = [rc.sb(f"zac{i}", [H, CHT], F32, group="b2") for i in range(2)]
    zrow = [rc.sb(f"zrow{i}", [1, CHT], F32, group="b2") for i in range(2)]
    onesr2 = rc.sb("onesr2", [1, 128], F32, group="b2")
    pb = rc.psum("pb", [128, CHT], group="b2ps")
    zscr = rc.dram("zscr", [2, H, CHT], F32)
    rc.dma("sync", cw[:], cmbw)
    rc.v(lambda e: e.memset(onesr2[:], 1.0), w=[onesr2])
    for ql in range(2):
        for u in range(NUNIT):
            zb = zsb[u % 2]
            rc.dma("sync", zb[:], slotsZ[u][0].rearrange("(h q) -> h q", h=H), tag=f"ldz{ql}{u}")
            if u == 0:
                rc.v(lambda e, zb=zb, ql=ql: e.tensor_scalar_mul(
                    out=zac[ql][:], in0=zb[:], scalar1=cw[0:H, ql * NUNIT:ql * NUNIT + 1]),
                    r=[zb, cw], w=[zac[ql]])
            else:
                rc.v(lambda e, zb=zb, ql=ql, u=u: e.scalar_tensor_tensor(
                    out=zac[ql][:], in0=zb[:], scalar=cw[0:H, ql * NUNIT + u:ql * NUNIT + u + 1],
                    in1=zac[ql][:], op0=ALU.mult, op1=ALU.add),
                    r=[zb, zac[ql], cw], w=[zac[ql]])
        rc.v(lambda e, ql=ql: e.reciprocal(out=zac[ql][:], in_=zac[ql][:]),
             r=[zac[ql]], w=[zac[ql]])
        rc.dma("gpsimd", zscr[ql], zac[ql][:], tag=f"stz{ql}")
    for rt in range(16):
        for ql in range(2):
            ac = acc[ql]
            for u in range(NUNIT):
                st_ = stile[(rt * 2 * NUNIT + ql * NUNIT + u) % 2]
                rc.dma("sync", st_[:], slots[u][rt * 128:(rt + 1) * 128, :], tag=f"lds{rt}{ql}{u}")
                if u == 0:
                    rc.v(lambda e, st_=st_, ac=ac, ql=ql: e.tensor_scalar_mul(
                        out=ac[:], in0=st_[:], scalar1=cw[:, ql * NUNIT:ql * NUNIT + 1]),
                        r=[st_, cw], w=[ac])
                else:
                    rc.v(lambda e, st_=st_, ac=ac, ql=ql, u=u: e.scalar_tensor_tensor(
                        out=ac[:], in0=st_[:], scalar=cw[:, ql * NUNIT + u:ql * NUNIT + u + 1],
                        in1=ac[:], op0=ALU.mult, op1=ALU.add), r=[st_, ac, cw], w=[ac])
            zr = zrow[ql]
            rc.dma("sync", zr[:], zscr[ql][rt:rt + 1, :], tag=f"ldzr{rt}{ql}")
            for i2 in range(2):
                rc.t(lambda e, zr=zr, i2=i2: e.matmul(
                    pb[:, i2 * 512:(i2 + 1) * 512], onesr2[:],
                    zr[0:1, i2 * 512:(i2 + 1) * 512],
                    start=True, stop=True), r=[onesr2, zr], w=[pb])
            rc.v(lambda e, rt=rt, ql=ql, ac=ac: e.tensor_mul(
                out=ctxf[rt][:, ql * CHT:(ql + 1) * CHT], in0=ac[:], in1=pb[:]),
                r=[ac, pb], w=[(f"ctxf{rt}", ql)], tag=f"cn{rt}")

    if upto <= 2:
        for rt in range(16):
            rc.dma("gpsimd", x1o[rt * 128:(rt + 1) * 128, :],
                   ctxf[rt][:], r=[(f"ctxf{rt}", 0), (f"ctxf{rt}", 1)], tag=f"dbgc{rt}")
        return rc.finish()

    rc.barrier()
    rc.free_group("b2ps")
    rc.free_group("b2")

    xnfm = [rc.sb(f"xnfm{i}", [128, R], BF16, group="b3k") for i in range(16)]
    for dc in range(16):
        rc.step("scalar", lambda e, dc=dc: e.dma_start(
            out=xnfm[dc][:], in_=xn_tm[:, dc * 128:(dc + 1) * 128], transpose=True),
            r=[xn_tm], w=[xnfm[dc]], dma=True, tag=f"btr{dc}")
    rc.s(lambda e: e.nop(), r=[xnfm[dc] for dc in range(16)], w=["xbar_quiet_b"], tag="xbq")
    w1sb = [rc.sb(f"w1sb{i}", [128, CHT], BF16, group="b3") for i in range(16)]
    b1sb = rc.sb("b1sb", [128, 64], F32, group="b3")
    hstg = rc.sb("hstg", [128, R], BF16, group="b3")
    ph = [rc.psum(f"ph{i}", [128, 512], group="b3ps") for i in range(2)]
    rc.dma("sync", b1sb[:], b1.rearrange("one (oc p) -> p (one oc)", p=128),
           r=["xbar_quiet_b"], allow_slow_non_contiguous=True)
    for ocg in range(8):
        for kc in range(16):
            rc.dma("gpsimd", w1sb[kc][:], w1[kc * 128:(kc + 1) * 128, ocg * CHT:(ocg + 1) * CHT],
                   r=["xbar_quiet_b"], tag=f"ldw1{ocg}")
        for oc2 in range(8):
            oc = ocg * 8 + oc2
            for tg in range(4):
                pp = ph[tg % 2]
                for kc in range(16):
                    rc.t(lambda e, kc=kc, oc2=oc2, tg=tg, pp=pp: e.matmul(
                        pp[:], w1sb[kc][:, oc2 * 128:(oc2 + 1) * 128],
                        xnfm[kc][:, tg * 512:(tg + 1) * 512], start=(kc == 0), stop=(kc == 15)),
                        r=[w1sb[kc], xnfm[kc]], w=[pp], tag=f"h{oc}")
                rc.a(lambda e, pp=pp, oc=oc, tg=tg: e.activation(
                    out=hstg[:, tg * 512:(tg + 1) * 512], in_=pp[:], func=AF.Gelu,
                    bias=b1sb[:, oc:oc + 1]), r=[pp, b1sb], w=[("hstg", tg)], tag=f"gel{oc}")
            rc.dma("gpsimd", hdram[oc * 128:(oc + 1) * 128, :], hstg[:],
                   r=[("hstg", tg) for tg in range(4)], tag=f"sth{oc}")

    if upto <= 3:
        return rc.finish()

    rc.barrier()
    rc.free_group("b3ps")
    rc.free_group("b3")
    rc.free_group("b3k")

    attn_dram = rc.dram("attn_dram", [R, D], F32)
    wosb = [rc.sb(f"wosb{i}", [128, D], BF16, group="b4a") for i in range(16)]
    ats = [rc.sb(f"ats{i}", [128, 512], F32, group="b4a") for i in range(2)]
    pat = rc.psum("pat", [128, 512], group="b4aps")
    for kc in range(16):
        rc.dma("gpsimd", wosb[kc][:], wo0[kc * 128:(kc + 1) * 128, :], tag=f"ldwo{kc}")
    for rt in range(16):
        for og in range(4):
            for kc in range(16):
                rc.t(lambda e, kc=kc, rt=rt, og=og: e.matmul(
                    pat[:], ctxf[kc][:, rt * 128:(rt + 1) * 128],
                    wosb[kc][:, og * 512:(og + 1) * 512], start=(kc == 0), stop=(kc == 15)),
                    r=[(f"ctxf{kc}", 0), (f"ctxf{kc}", 1), wosb[kc]], w=[pat], tag=f"at{og}{rt}")
            ab = ats[og % 2]
            rc.v(lambda e, ab=ab: e.tensor_copy(out=ab[:], in_=pat[:]), r=[pat], w=[ab])
            rc.dma("gpsimd", attn_dram[rt * 128:(rt + 1) * 128, og * 512:(og + 1) * 512],
                   ab[:], tag=f"stat{og}{rt}")
    rc.barrier()
    rc.free_group("b4aps")
    rc.free_group("b4a")
    rc.free_group("ctxfg")

    bosb = rc.sb("bosb", [128, D], F32, group="b45")
    b2sb = rc.sb("b2sb", [128, D], F32, group="b45")
    onesr3 = rc.sb("onesr3", [1, 128], F32, group="b45")
    borow = rc.sb("borow", [1, D], F32, group="b45")
    b2row = rc.sb("b2row", [1, D], F32, group="b45")
    w2sb = [rc.sb(f"w2sb{i}", [128, 512], BF16, group="b45") for i in range(64)]
    hrt = [rc.sb(f"hrt{i}", [128, DFF], BF16, group="b45") for i in range(2)]
    xot = [rc.sb(f"xot{i}", [128, 512], F32, group="b45") for i in range(2)]
    xnt = [rc.sb(f"xnt{i}", [128, 512], BF16, group="b45") for i in range(2)]
    att = [rc.sb(f"att{i}", [128, 512], F32, group="b45") for i in range(2)]
    ew = [rc.sb(f"ew{i}", [128, 512], F32, group="b45") for i in range(4)]
    x1st = [rc.sb(f"x1st{i}", [128, 512], F32, group="b45") for i in range(2)]
    py = rc.psum("py", [128, 512], group="b45ps")
    pbc2 = rc.psum("pbc2", [128, D], group="b45ps")
    rc.v(lambda e: e.memset(onesr3[:], 1.0), w=[onesr3])
    rc.dma("sync", borow[:], bo)
    rc.dma("sync", b2row[:], b2)
    for i4 in range(4):
        rc.t(lambda e, i4=i4: e.matmul(pbc2[:, i4 * 512:(i4 + 1) * 512], onesr3[:],
                                       borow[:, i4 * 512:(i4 + 1) * 512], start=True, stop=True),
             r=[onesr3, borow], w=[pbc2])
    rc.v(lambda e: e.tensor_copy(out=bosb[:], in_=pbc2[:]), r=[pbc2], w=[bosb])
    for i4 in range(4):
        rc.t(lambda e, i4=i4: e.matmul(pbc2[:, i4 * 512:(i4 + 1) * 512], onesr3[:],
                                       b2row[:, i4 * 512:(i4 + 1) * 512], start=True, stop=True),
             r=[onesr3, b2row], w=[pbc2])
    rc.v(lambda e: e.tensor_copy(out=b2sb[:], in_=pbc2[:]), r=[pbc2], w=[b2sb])
    for og in range(4):
        for kc in range(64):
            rc.dma("gpsimd", w2sb[kc][:], w2[kc * 128:(kc + 1) * 128, og * 512:(og + 1) * 512],
                   tag=f"ldw2{og}")
        for rt in range(16):
            hb = hrt[rt % 2]
            extra = [w2sb[kc] for kc in range(64)] if rt == 0 else []
            rc.dma("sync", hb[:].rearrange("p (c t) -> p c t", c=64),
                   hdram[:, rt * 128:(rt + 1) * 128].rearrange("(c p) t -> p c t", p=128),
                   r=extra, tag=f"ldh{og}{rt}")
            xob = xot[rt % 2]
            xnb = xnt[rt % 2]
            atb = att[rt % 2]
            rc.dma("sync", xob[:], xo[rt * 128:(rt + 1) * 128, og * 512:(og + 1) * 512],
                   tag=f"ldxo{og}{rt}")
            rc.dma("sync", xnb[:], xn_tm[rt * 128:(rt + 1) * 128, og * 512:(og + 1) * 512],
                   tag=f"ldxn{og}{rt}")
            rc.dma("sync", atb[:], attn_dram[rt * 128:(rt + 1) * 128, og * 512:(og + 1) * 512],
                   tag=f"ldat{og}{rt}")
            for kc in range(64):
                rc.t(lambda e, kc=kc, rt=rt, hb=hb: e.matmul(
                    py[:], hb[:, kc * 128:(kc + 1) * 128], w2sb[kc][:],
                    start=(kc == 0), stop=(kc == 63)), r=[hb, w2sb[kc]], w=[py], tag=f"y{og}{rt}")
            e0, e1, e2, e3 = ew
            x1b = x1st[rt % 2]
            rc.v(lambda e, xob=xob, atb=atb: e.tensor_add(out=e0[:], in0=atb[:], in1=xob[:]),
                 r=[atb, xob], w=[e0])
            rc.v(lambda e, xnb=xnb: e.tensor_add(out=e1[:], in0=e0[:], in1=xnb[:]),
                 r=[e0, xnb], w=[e1])
            rc.v(lambda e, og=og: e.tensor_add(out=e2[:], in0=e1[:],
                                               in1=bosb[:, og * 512:(og + 1) * 512]),
                 r=[e1, bosb], w=[e2])
            rc.v(lambda e, og=og: e.tensor_add(out=e3[:], in0=py[:],
                                               in1=b2sb[:, og * 512:(og + 1) * 512]),
                 r=[py, b2sb], w=[e3])
            rc.v(lambda e, rt=rt, x1b=x1b, e3=e3, e2=e2: e.scalar_tensor_tensor(
                out=x1b[:], in0=e3[:], scalar=w_own[:, rt:rt + 1], in1=e2[:],
                op0=ALU.mult, op1=ALU.add), r=[e3, e2, w_own], w=[x1b])
            rc.dma("gpsimd", x1o[rt * 128:(rt + 1) * 128, og * 512:(og + 1) * 512], x1b[:],
                   tag=f"stx1{og}{rt}")

    if upto <= 4:
        return rc.finish()

    rc.barrier()
    rc.free_group("b45ps")
    rc.free_group("b45")
    for i in range(16):
        pass

    x_sb = [rc.sb(f"bx_sb{i}", [128, D], F32, group="b6") for i in range(2)]
    sq = rc.sb("bsq", [128, D], F32, group="b6")
    s1 = rc.sb("bs1", [128, 1], F32, group="b6")
    mmean = rc.sb("bmmean", [128, 1], F32, group="b6")
    var = rc.sb("bvar", [128, 1], F32, group="b6")
    rstd = rc.sb("brstd", [128, 1], F32, group="b6")
    xn_sb = [rc.sb(f"bxn_sb{i}", [128, D], BF16, group="b6") for i in range(2)]
    eps_sb = rc.sb("beps", [128, 1], F32, group="b6")
    rc.v(lambda e: e.memset(eps_sb[:], float(EPS)), w=[eps_sb])
    for rt in range(NT):
        xt = x_sb[rt % 2]
        xn = xn_sb[rt % 2]
        rc.dma("sync", xt[:], x1o[rt * 128:(rt + 1) * 128, :], tag=f"bldx{rt}")
        rc.a(lambda e, xt=xt: e.activation(out=sq[:], in_=xt[:], func=AF.Square),
             r=[xt], w=[sq])
        rc.v(lambda e, xt=xt: e.reduce_sum(out=s1[:], in_=xt[:], axis=AX.X), r=[xt], w=[s1])
        rc.v(lambda e: e.tensor_scalar_mul(out=mmean[:], in0=s1[:], scalar1=1.0 / D),
             r=[s1], w=[mmean])
        rc.v(lambda e: e.reduce_sum(out=s1[:], in_=sq[:], axis=AX.X), r=[sq], w=[s1])
        rc.v(lambda e: e.tensor_mul(out=var[:], in0=mmean[:], in1=mmean[:]), r=[mmean], w=[var])
        rc.v(lambda e: e.tensor_scalar(out=var[:], in0=s1[:], scalar1=1.0 / D,
                                       scalar2=var[:, 0:1], op0=ALU.mult, op1=ALU.subtract),
             r=[s1, var], w=[var])
        rc.a(lambda e: e.activation(out=var[:], in_=var[:], func=AF.Ln, bias=eps_sb[:, 0:1]),
             r=[var, eps_sb], w=[var])
        rc.a(lambda e: e.activation(out=rstd[:], in_=var[:], func=AF.Exp, scale=-0.5),
             r=[var], w=[rstd])
        rc.v(lambda e, xt=xt, xn=xn: e.tensor_scalar(
            out=xn[:], in0=xt[:], scalar1=mmean[:, 0:1], scalar2=rstd[:, 0:1],
            op0=ALU.subtract, op1=ALU.mult), r=[xt, mmean, rstd], w=[xn])
        rc.dma("gpsimd", xn1_tm[rt * 128:(rt + 1) * 128, :], xn[:], tag=f"bstxn{rt}")

    if upto <= 5:
        return rc.finish()

    rc.barrier()
    rc.free_group("b6")

    xn1fm = [rc.sb(f"xn1fm{i}", [128, R], BF16, group="b7") for i in range(16)]
    for dc in range(16):
        rc.step("scalar", lambda e, dc=dc: e.dma_start(
            out=xn1fm[dc][:], in_=xn1_tm[:, dc * 128:(dc + 1) * 128], transpose=True),
            r=[xn1_tm], w=[xn1fm[dc]], dma=True, tag=f"b7tr{dc}")
    rc.s(lambda e: e.nop(), r=[xn1fm[dc] for dc in range(16)], w=["xbar_quiet7"], tag="xbq7")
    wsec = [rc.sb(f"wsec{i}", [128, D], BF16, group="b7") for i in range(16)]
    cos_sb = rc.sb("cos_sb", [128, R], F32, group="b7")
    sin_sb = rc.sb("sin_sb", [128, R], F32, group="b7")
    rR = rc.sb("rR", [128, 128], BF16, group="b7")
    qtile = rc.sb("qtile", [128, 512], BF16, group="b7")
    r1 = rc.sb("r1", [128, 512], F32, group="b7")
    r2 = rc.sb("r2", [128, 512], F32, group="b7")
    ostg2 = [rc.sb(f"ostg2_{i}", [128, R], BF16, group="b7") for i in range(2)]
    pq = [rc.psum(f"pq{i}", [128, 512], group="b7ps") for i in range(2)]
    prot = rc.psum("prot", [128, 512], group="b7ps")
    rc.dma("sync", cos_sb[:], cosq, r=["xbar_quiet7"])
    rc.dma("sync", sin_sb[:], sinq, r=["xbar_quiet7"])
    rc.dma("gpsimd", rR[:], ropeR, r=["xbar_quiet7"])
    for sec, outdram, do_rope in ((0, q1, True), (1, k1, True)):
        for kc in range(16):
            rc.dma("gpsimd", wsec[kc][:], wqkvT[kc * 128:(kc + 1) * 128, sec * D:(sec + 1) * D],
                   r=["xbar_quiet7"], tag=f"ldwsec{sec}")
        for oc in range(16):
            stg = ostg2[oc % 2]
            for tg in range(4):
                pp = pq[tg % 2]
                for kc in range(16):
                    rc.t(lambda e, kc=kc, oc=oc, tg=tg, pp=pp: e.matmul(
                        pp[:], wsec[kc][:, oc * 128:(oc + 1) * 128],
                        xn1fm[kc][:, tg * 512:(tg + 1) * 512], start=(kc == 0), stop=(kc == 15)),
                        r=[wsec[kc], xn1fm[kc]], w=[pp], tag=f"qk{sec}{oc}")
                rc.v(lambda e, pp=pp: e.tensor_copy(out=qtile[:], in_=pp[:]), r=[pp], w=[qtile])
                rc.t(lambda e: e.matmul(prot[:], rR[:], qtile[:], start=True, stop=True),
                     r=[rR, qtile], w=[prot])
                rc.v(lambda e, tg=tg: e.tensor_mul(out=r1[:], in0=qtile[:],
                                                   in1=cos_sb[:, tg * 512:(tg + 1) * 512]),
                     r=[qtile, cos_sb], w=[r1])
                rc.v(lambda e, tg=tg: e.tensor_mul(out=r2[:], in0=prot[:],
                                                   in1=sin_sb[:, tg * 512:(tg + 1) * 512]),
                     r=[prot, sin_sb], w=[r2])
                rc.v(lambda e, tg=tg, stg=stg: e.tensor_add(
                    out=stg[:, tg * 512:(tg + 1) * 512], in0=r1[:], in1=r2[:]),
                    r=[r1, r2], w=[(stg.name, tg)])
            rc.dma("gpsimd", outdram[oc * 128:(oc + 1) * 128, :], stg[:],
                   r=[(stg.name, tg) for tg in range(4)], tag=f"stqk{sec}{oc}")
    for kc in range(16):
        rc.dma("gpsimd", wsec[kc][:], wqkvT[kc * 128:(kc + 1) * 128, 2 * D:3 * D],
               tag="ldwsecv")
    for tt in range(NT):
        stg = ostg2[tt % 2]
        for vg in range(4):
            pp = pq[vg % 2]
            for kc in range(16):
                rc.t(lambda e, kc=kc, tt=tt, vg=vg, pp=pp: e.matmul(
                    pp[:], xn1fm[kc][:, tt * 128:(tt + 1) * 128],
                    wsec[kc][:, vg * 512:(vg + 1) * 512], start=(kc == 0), stop=(kc == 15)),
                    r=[wsec[kc], xn1fm[kc]], w=[pp], tag=f"v1{tt}")
            rc.v(lambda e, pp=pp, vg=vg, stg=stg: e.tensor_copy(
                out=stg[:, vg * 512:(vg + 1) * 512], in_=pp[:]), r=[pp], w=[(stg.name, vg)])
        rc.dma("gpsimd", v1[tt * 128:(tt + 1) * 128, :], stg[:],
               r=[(stg.name, vg) for vg in range(4)], tag=f"stv1{tt}")
    return rc.finish()



def build_launch_c(upto=99):
    rc = Rec()
    x1i = rc.dram("x1i", [R, D], F32, kind="ExternalInput")
    xn1_tm = rc.dram("xn1_tm", [R, D], BF16, kind="ExternalInput")
    q1s = rc.dram("q1s", [NUNIT, D, CHT], BF16, kind="ExternalInput")
    k1s = rc.dram("k1s", [NUNIT, D, CHT], BF16, kind="ExternalInput")
    v1s = rc.dram("v1s", [NUNIT, CHT, D], BF16, kind="ExternalInput")
    masks = rc.dram("masks", [8, 128, 512], BF16, kind="ExternalInput")
    cmbw = rc.dram("cmbw", [128, 2 * NUNIT], F32, kind="ExternalInput")
    wo1T = rc.dram("wo1T", [D, D], F32, kind="ExternalInput")
    wqgT = rc.dram("wqgT", [D, HID], I32, kind="ExternalInput")
    wquT = rc.dram("wquT", [D, HID], I32, kind="ExternalInput")
    wqdT = rc.dram("wqdT", [HID, D], I32, kind="ExternalInput")
    sg = rc.dram("sg", [1, HID], F32, kind="ExternalInput")
    su = rc.dram("su", [1, HID], F32, kind="ExternalInput")
    sd = rc.dram("sd", [1, D], F32, kind="ExternalInput")
    out = rc.dram("out", [R, D], F32, kind="ExternalOutput")

    slots = rc.dram("slots", [NUNIT, D, CHT], F32)
    slotsZ = rc.dram("slotsZ", [NUNIT, 1, H * CHT], F32)
    gu_dram = rc.dram("gu_dram", [HID, R], BF16)
    attn_dram = rc.dram("attn_dram", [R, D], F32)
    gcs_dram = rc.dram("gcs_dram", [1, R], F32)
    cs_dram = rc.dram("cs_dram", [1, R], F32)

    onesbf = rc.sb("onesbf", [128, 1], BF16)
    qsb = rc.sb("qsb", [128, H * CHT], BF16, group="c1")
    ksb = rc.sb("ksb", [128, H * CHT], BF16, group="c1")
    vsb = rc.sb("vsb", [128, 8 * D], BF16, group="c1")
    msk = rc.sb("msk", [128, 8 * 512], BF16, group="c1")
    psb = rc.sb("psb", [128, 512], BF16, group="c1")
    cst = rc.sb("cst", [128, 512], F32, group="c1")
    zst = rc.sb("zst", [1, H * CHT], F32, group="c1")
    psc = [rc.psum(f"psc{i}", [128, 512], group="c1ps") for i in range(2)]
    pctx = rc.psum("pctx", [128, 512], group="c1ps")
    pz = rc.psum("pz", [1, 512], group="c1ps")
    rc.dma("sync", msk[:].rearrange("p (m q) -> p m q", m=8), masks.rearrange("m p q -> p m q"))
    rc.v(lambda e: e.memset(onesbf[:], 1.0), w=[onesbf])
    for u in range(NUNIT):
        diag = u < 2
        rc.dma("sync", qsb[:].rearrange("p (h q) -> p h q", h=H),
               q1s[u].rearrange("(h p) q -> p h q", p=128), tag=f"ldq{u}")
        rc.dma("sync", ksb[:].rearrange("p (h q) -> p h q", h=H),
               k1s[u].rearrange("(h p) q -> p h q", p=128), tag=f"ldk{u}")
        rc.dma("sync", vsb[:].rearrange("p (a v) -> p a v", a=8),
               v1s[u].rearrange("(a p) v -> p a v", p=128), tag=f"ldv{u}")
        sci = 0
        for h in range(H):
            for qg in range(2):
                kts = list(range(4)) if (diag and qg == 0) else list(range(8))
                nk = len(kts)
                for i, kt in enumerate(kts):
                    pp = psc[sci % 2]
                    sci += 1
                    rc.t(lambda e, h=h, kt=kt, qg=qg, pp=pp: e.matmul(
                        pp[:], ksb[:, h * CHT + kt * 128: h * CHT + (kt + 1) * 128],
                        qsb[:, h * CHT + qg * 512: h * CHT + (qg + 1) * 512],
                        start=True, stop=True), r=[ksb, qsb], w=[pp], tag=f"sc{u}")
                    rc.a(lambda e, pp=pp: e.activation(out=psb[:], in_=pp[:], func=AF.Exp,
                                                       scale=SCL), r=[pp], w=[psb], tag=f"exp{u}")
                    if diag and ((qg == 0 and kt < 4) or (qg == 1 and kt >= 4)):
                        rc.v(lambda e, mi=kt: e.tensor_mul(
                            out=psb[:], in0=psb[:], in1=msk[:, mi * 512:(mi + 1) * 512]),
                            r=[psb, msk], w=[psb], tag=f"mask{u}")
                    rc.t(lambda e, kt=kt, h=h, i=i, nk=nk: e.matmul(
                        pctx[:], vsb[:, kt * D + h * 128: kt * D + (h + 1) * 128], psb[:],
                        start=(i == 0), stop=(i == nk - 1)), r=[vsb, psb], w=[pctx], tag=f"cx{u}")
                    rc.t(lambda e, i=i, nk=nk: e.matmul(
                        pz[:], onesbf[:], psb[:], start=(i == 0), stop=(i == nk - 1)),
                        r=[onesbf, psb], w=[pz], tag=f"z{u}")
                rc.v(lambda e: e.tensor_copy(out=cst[:], in_=pctx[:]), r=[pctx], w=[cst])
                rc.v(lambda e, h=h, qg=qg: e.tensor_copy(
                    out=zst[0:1, h * CHT + qg * 512: h * CHT + (qg + 1) * 512], in_=pz[:]),
                    r=[pz], w=[("zst", h, qg)])
                rc.dma("gpsimd", slots[u][h * 128:(h + 1) * 128, qg * 512:(qg + 1) * 512],
                       cst[:], tag=f"stc{u}")
        rc.dma("gpsimd", slotsZ[u][:, :], zst[:],
               r=[("zst", h, qg) for h in range(H) for qg in range(2)], tag=f"stz{u}")

    if upto <= 1:
        return rc.finish()
    rc.barrier()
    rc.free_group("c1ps")
    rc.free_group("c1")

    ctxf = [rc.sb(f"ctxf{i}", [128, R], BF16, group="ctxfg") for i in range(16)]
    cw = rc.sb("cw", [128, 2 * NUNIT], F32, group="c2")
    stile = [rc.sb(f"stile{i}", [128, CHT], F32, group="c2") for i in range(2)]
    acc = [rc.sb(f"acc{i}", [128, CHT], F32, group="c2") for i in range(2)]
    zsb = [rc.sb(f"zsb{i}", [H, CHT], F32, group="c2") for i in range(2)]
    zac = [rc.sb(f"zac{i}", [H, CHT], F32, group="c2") for i in range(2)]
    zrow = [rc.sb(f"zrow{i}", [1, CHT], F32, group="c2") for i in range(2)]
    onesr2 = rc.sb("onesr2", [1, 128], F32, group="c2")
    pb = rc.psum("pb", [128, CHT], group="c2ps")
    zscr = rc.dram("zscr", [2, H, CHT], F32)
    rc.dma("sync", cw[:], cmbw)
    rc.v(lambda e: e.memset(onesr2[:], 1.0), w=[onesr2])
    for ql in range(2):
        for u in range(NUNIT):
            zb = zsb[u % 2]
            rc.dma("sync", zb[:], slotsZ[u][0].rearrange("(h q) -> h q", h=H), tag=f"ldz{ql}{u}")
            if u == 0:
                rc.v(lambda e, zb=zb, ql=ql: e.tensor_scalar_mul(
                    out=zac[ql][:], in0=zb[:], scalar1=cw[0:H, ql * NUNIT:ql * NUNIT + 1]),
                    r=[zb, cw], w=[zac[ql]])
            else:
                rc.v(lambda e, zb=zb, ql=ql, u=u: e.scalar_tensor_tensor(
                    out=zac[ql][:], in0=zb[:], scalar=cw[0:H, ql * NUNIT + u:ql * NUNIT + u + 1],
                    in1=zac[ql][:], op0=ALU.mult, op1=ALU.add),
                    r=[zb, zac[ql], cw], w=[zac[ql]])
        rc.v(lambda e, ql=ql: e.reciprocal(out=zac[ql][:], in_=zac[ql][:]),
             r=[zac[ql]], w=[zac[ql]])
        rc.dma("gpsimd", zscr[ql], zac[ql][:], tag=f"stzc{ql}")
    for rt in range(16):
        for ql in range(2):
            ac = acc[ql]
            for u in range(NUNIT):
                st_ = stile[(rt * 2 * NUNIT + ql * NUNIT + u) % 2]
                rc.dma("sync", st_[:], slots[u][rt * 128:(rt + 1) * 128, :], tag=f"lds{rt}{ql}{u}")
                if u == 0:
                    rc.v(lambda e, st_=st_, ac=ac, ql=ql: e.tensor_scalar_mul(
                        out=ac[:], in0=st_[:], scalar1=cw[:, ql * NUNIT:ql * NUNIT + 1]),
                        r=[st_, cw], w=[ac])
                else:
                    rc.v(lambda e, st_=st_, ac=ac, ql=ql, u=u: e.scalar_tensor_tensor(
                        out=ac[:], in0=st_[:], scalar=cw[:, ql * NUNIT + u:ql * NUNIT + u + 1],
                        in1=ac[:], op0=ALU.mult, op1=ALU.add), r=[st_, ac, cw], w=[ac])
            zr = zrow[ql]
            rc.dma("sync", zr[:], zscr[ql][rt:rt + 1, :], tag=f"ldzr{rt}{ql}")
            for i2 in range(2):
                rc.t(lambda e, zr=zr, i2=i2: e.matmul(
                    pb[:, i2 * 512:(i2 + 1) * 512], onesr2[:],
                    zr[0:1, i2 * 512:(i2 + 1) * 512], start=True, stop=True),
                    r=[onesr2, zr], w=[pb])
            rc.v(lambda e, rt=rt, ql=ql, ac=ac: e.tensor_mul(
                out=ctxf[rt][:, ql * CHT:(ql + 1) * CHT], in0=ac[:], in1=pb[:]),
                r=[ac, pb], w=[(f"ctxf{rt}", ql)], tag=f"cn{rt}")
    rc.barrier()
    rc.free_group("c2ps")
    rc.free_group("c2")

    wosb = [rc.sb(f"wosb{i}", [128, D], BF16, group="c3") for i in range(16)]
    ats = [rc.sb(f"ats{i}", [128, 512], F32, group="c3") for i in range(2)]
    pat = rc.psum("pat", [128, 512], group="c3ps")
    for kc in range(16):
        rc.dma("gpsimd", wosb[kc][:], wo1T[kc * 128:(kc + 1) * 128, :], tag=f"ldwo{kc}")
    for rt in range(16):
        for og in range(4):
            for kc in range(16):
                rc.t(lambda e, kc=kc, rt=rt, og=og: e.matmul(
                    pat[:], ctxf[kc][:, rt * 128:(rt + 1) * 128],
                    wosb[kc][:, og * 512:(og + 1) * 512], start=(kc == 0), stop=(kc == 15)),
                    r=[(f"ctxf{kc}", 0), (f"ctxf{kc}", 1), wosb[kc]], w=[pat], tag=f"at{og}{rt}")
            ab = ats[og % 2]
            rc.v(lambda e, ab=ab: e.tensor_copy(out=ab[:], in_=pat[:]), r=[pat], w=[ab])
            rc.dma("gpsimd", attn_dram[rt * 128:(rt + 1) * 128, og * 512:(og + 1) * 512],
                   ab[:], tag=f"stat{og}{rt}")
    if upto <= 3:
        for rt in range(16):
            rc.dma("gpsimd", out[rt * 128:(rt + 1) * 128, :],
                   attn_dram[rt * 128:(rt + 1) * 128, :], tag=f"dbga{rt}")
        return rc.finish()
    rc.barrier()
    rc.free_group("c3ps")
    rc.free_group("c3")
    rc.free_group("ctxfg")

    xn1fm = [rc.sb(f"xn1fm{i}", [128, R], BF16, group="c4k") for i in range(16)]
    for dc in range(16):
        rc.step("scalar", lambda e, dc=dc: e.dma_start(
            out=xn1fm[dc][:], in_=xn1_tm[:, dc * 128:(dc + 1) * 128], transpose=True),
            r=[xn1_tm], w=[xn1fm[dc]], dma=True, tag=f"ctr{dc}")
    rc.s(lambda e: e.nop(), r=[xn1fm[dc] for dc in range(16)], w=["xbar_quiet_c"], tag="xbqc")
    sgsb = rc.sb("sgsb", [128, 44], F32, group="c4")
    susb = rc.sb("susb", [128, 44], F32, group="c4")
    n128 = rc.sb("n128", [1, 128], F32, group="c4")
    cs_sb = rc.sb("cs_sb", [1, R], F32, group="c4")
    wg_sb = [rc.sb(f"wg_sb{i}", [128, D], BF16, group="c4") for i in range(2)]
    wu_sb = [rc.sb(f"wu_sb{i}", [128, D], BF16, group="c4") for i in range(2)]
    gt = rc.sb("gt", [128, 512], BF16, group="c4")
    ut = rc.sb("ut", [128, 512], BF16, group="c4")
    gut = rc.sb("gut", [128, 512], BF16, group="c4")
    pg = rc.psum("pg", [128, 512], group="c4ps")
    pu = rc.psum("pu", [128, 512], group="c4ps")
    pcs = rc.psum("pcs", [1, 512], group="c4ps")
    pz1 = rc.psum("pz1", [1, 512], group="c4ps")
    gcs_acc = rc.sb("gcs_acc", [1, R], F32, group="c4")
    rc.dma("sync", sgsb[:], sg.rearrange("one (hc p) -> p (one hc)", p=128),
           r=["xbar_quiet_c"], allow_slow_non_contiguous=True)
    rc.dma("sync", susb[:], su.rearrange("one (hc p) -> p (one hc)", p=128),
           r=["xbar_quiet_c"], allow_slow_non_contiguous=True)
    rc.v(lambda e: e.memset(n128[:], -128.0), w=[n128])
    rc.v(lambda e: e.memset(gcs_acc[:], 0.0), w=[gcs_acc])
    for tg in range(4):
        for kc in range(16):
            rc.t(lambda e, tg=tg, kc=kc: e.matmul(
                pcs[:], onesbf[:],
                xn1fm[kc][:, tg * 512:(tg + 1) * 512], start=(kc == 0), stop=(kc == 15)),
                r=[onesbf, xn1fm[kc], "xbar_quiet_c"], w=[pcs], tag="cs")
        rc.v(lambda e, tg=tg: e.tensor_copy(out=cs_sb[0:1, tg * 512:(tg + 1) * 512],
                                            in_=pcs[:]), r=[pcs], w=[cs_sb])
    for hc in range(44):
        wg = wg_sb[hc % 2]
        wu = wu_sb[hc % 2]
        rc.dma("gpsimd", wg[:].rearrange("p (c h) -> p c h", c=16),
               wqgT[:, hc * 128:(hc + 1) * 128].rearrange("(c p) h -> p c h", p=128),
               r=["xbar_quiet_c"], tag=f"ldwg{hc}")
        rc.dma("gpsimd", wu[:].rearrange("p (c h) -> p c h", c=16),
               wquT[:, hc * 128:(hc + 1) * 128].rearrange("(c p) h -> p c h", p=128),
               r=["xbar_quiet_c"], tag=f"ldwu{hc}")
        for tg in range(4):
            for kc in range(16):
                rc.t(lambda e, kc=kc, tg=tg, wg=wg: e.matmul(
                    pg[:], wg[:, kc * 128:(kc + 1) * 128],
                    xn1fm[kc][:, tg * 512:(tg + 1) * 512], start=(kc == 0), stop=False),
                    r=[wg, xn1fm[kc]], w=[pg], tag=f"g{hc}")
            rc.t(lambda e, tg=tg: e.matmul(
                pg[:], n128[:], cs_sb[0:1, tg * 512:(tg + 1) * 512], start=False, stop=True),
                r=[n128, cs_sb], w=[pg], tag=f"gc{hc}")
            for kc in range(16):
                rc.t(lambda e, kc=kc, tg=tg, wu=wu: e.matmul(
                    pu[:], wu[:, kc * 128:(kc + 1) * 128],
                    xn1fm[kc][:, tg * 512:(tg + 1) * 512], start=(kc == 0), stop=False),
                    r=[wu, xn1fm[kc]], w=[pu], tag=f"u{hc}")
            rc.t(lambda e, tg=tg: e.matmul(
                pu[:], n128[:], cs_sb[0:1, tg * 512:(tg + 1) * 512], start=False, stop=True),
                r=[n128, cs_sb], w=[pu], tag=f"uc{hc}")
            rc.a(lambda e, hc=hc: e.activation(out=gt[:], in_=pg[:], func=AF.Silu,
                                               scale=sgsb[:, hc:hc + 1]),
                 r=[pg, sgsb], w=[gt], tag=f"sil{hc}")
            rc.a(lambda e, hc=hc: e.activation(out=ut[:], in_=pu[:], func=AF.Copy,
                                               scale=susb[:, hc:hc + 1]),
                 r=[pu, susb], w=[ut], tag=f"cpu{hc}")
            rc.v(lambda e: e.tensor_mul(out=gut[:], in0=gt[:], in1=ut[:]),
                 r=[gt, ut], w=[gut], tag=f"gu{hc}")
            rc.t(lambda e: e.matmul(pz1[:], onesbf[:], gut[:], start=True, stop=True),
                 r=[onesbf, gut], w=[pz1], tag=f"gcs{hc}")
            rc.v(lambda e, tg=tg: e.scalar_tensor_tensor(
                out=gcs_acc[0:1, tg * 512:(tg + 1) * 512], in0=pz1[:], scalar=1.0,
                in1=gcs_acc[0:1, tg * 512:(tg + 1) * 512], op0=ALU.mult, op1=ALU.add),
                r=[pz1, ("gcs_acc", tg)], w=[("gcs_acc", tg)], tag=f"gca{hc}")
            rc.dma("gpsimd", gu_dram[hc * 128:(hc + 1) * 128, tg * 512:(tg + 1) * 512],
                   gut[:], tag=f"stgu{hc}")
    rc.dma("gpsimd", gcs_dram, gcs_acc[:],
           r=[("gcs_acc", tg) for tg in range(4)], w=["stgcs"], tag="stgcs")
    if upto <= 4:
        for rt in range(15):
            rc.dma("gpsimd", out[rt * 128:(rt + 1) * 128, :],
                   gu_dram[rt * 128:(rt + 1) * 128, :], tag=f"dbgu{rt}")
        rc.dma("gpsimd", out[2047:2048, :], gcs_dram, r=["stgcs"], tag="dbggcs")
        rc.dma("gpsimd", out[2046:2047, :], cs_dram if False else gcs_dram, tag="dbggcs2")
        return rc.finish()
    rc.barrier()
    rc.free_group("c4ps")
    rc.free_group("c4")
    rc.free_group("c4k")

    sdbc = rc.sb("sdbc", [128, D], F32, group="c5")
    sdrow = rc.sb("sdrow", [1, D], F32, group="c5")
    onesr4 = rc.sb("onesr4", [1, 128], F32, group="c5")
    n128row = rc.sb("n128row", [1, 512], F32, group="c5")
    gcs2 = rc.sb("gcs2", [1, R], F32, group="c5")
    wd_sb = [rc.sb(f"wd_sb{i}", [128, 512], BF16, group="c5") for i in range(44)]
    gurt = [rc.sb(f"gurt{i}", [128, HID], BF16, group="c5") for i in range(2)]
    x1t = [rc.sb(f"x1t{i}", [128, 512], F32, group="c5") for i in range(2)]
    att2 = [rc.sb(f"att2{i}", [128, 512], F32, group="c5") for i in range(2)]
    ew2 = [rc.sb(f"ew2_{i}", [128, 512], F32, group="c5") for i in range(2)]
    ost = [rc.sb(f"ost{i}", [128, 512], F32, group="c5") for i in range(2)]
    pd = rc.psum("pd", [128, 512], group="c5ps")
    pbc3 = rc.psum("pbc3", [128, D], group="c5ps")
    rc.v(lambda e: e.memset(onesr4[:], 1.0), w=[onesr4])
    rc.v(lambda e: e.memset(n128row[:], -128.0), w=[n128row])
    rc.dma("sync", sdrow[:], sd)
    rc.dma("sync", gcs2[:], gcs_dram)
    for i4 in range(4):
        rc.t(lambda e, i4=i4: e.matmul(pbc3[:, i4 * 512:(i4 + 1) * 512], onesr4[:],
                                       sdrow[:, i4 * 512:(i4 + 1) * 512], start=True, stop=True),
             r=[onesr4, sdrow], w=[pbc3])
    rc.v(lambda e: e.tensor_copy(out=sdbc[:], in_=pbc3[:]), r=[pbc3], w=[sdbc])
    rc.t(lambda e: e.matmul(pd[:], gcs2[0:1, 0:128], n128row[:], start=True, stop=True),
         r=[gcs2, n128row], w=[pd], tag="warmd")
    rc.v(lambda e: e.tensor_copy(out=ew2[0][:], in_=pd[:]), r=[pd], w=[ew2[0]], tag="warmdev")
    for og in range(4):
        for kc in range(44):
            rc.dma("gpsimd", wd_sb[kc][:], wqdT[kc * 128:(kc + 1) * 128, og * 512:(og + 1) * 512],
                   tag=f"ldwd{og}")
        for rt in range(16):
            gb = gurt[rt % 2]
            extra = [wd_sb[kc] for kc in range(44)] if rt == 0 else []
            rc.dma("sync", gb[:].rearrange("p (c t) -> p c t", c=44),
                   gu_dram[:, rt * 128:(rt + 1) * 128].rearrange("(c p) t -> p c t", p=128),
                   r=extra, tag=f"ldgu{og}{rt}")
            x1b = x1t[rt % 2]
            atb = att2[rt % 2]
            rc.dma("sync", x1b[:], x1i[rt * 128:(rt + 1) * 128, og * 512:(og + 1) * 512],
                   tag=f"ldx1{og}{rt}")
            rc.dma("sync", atb[:], attn_dram[rt * 128:(rt + 1) * 128, og * 512:(og + 1) * 512],
                   tag=f"ldat{og}{rt}")
            for kc in range(44):
                rc.t(lambda e, kc=kc, gb=gb: e.matmul(
                    pd[:], gb[:, kc * 128:(kc + 1) * 128], wd_sb[kc][:],
                    start=(kc == 0), stop=False), r=[gb, wd_sb[kc]], w=[pd], tag=f"d{og}{rt}")
            rc.t(lambda e, rt=rt: e.matmul(
                pd[:], gcs2[0:1, rt * 128:(rt + 1) * 128], n128row[:],
                start=False, stop=True), r=[n128row, gcs2], w=[pd], tag=f"dc{og}{rt}")
            e0, e1 = ew2
            ob = ost[rt % 2]
            rc.v(lambda e, og=og: e.tensor_mul(out=e0[:], in0=pd[:],
                                               in1=sdbc[:, og * 512:(og + 1) * 512]),
                 r=[pd, sdbc], w=[e0])
            rc.v(lambda e, x1b=x1b: e.tensor_add(out=e1[:], in0=e0[:], in1=x1b[:]),
                 r=[e0, x1b], w=[e1])
            rc.v(lambda e, atb=atb, ob=ob: e.tensor_add(out=ob[:], in0=e1[:], in1=atb[:]),
                 r=[e1, atb], w=[ob])
            rc.dma("gpsimd", out[rt * 128:(rt + 1) * 128, og * 512:(og + 1) * 512], ob[:],
                   tag=f"sto{og}{rt}")
    return rc.finish()



def _mk_masks():
    masks = np.zeros((8, 128, 512), np.float32)
    for m in range(8):
        qg, kt = (0, m) if m < 4 else (1, m)
        kk = kt * 128 + np.arange(128)[:, None]
        qq = qg * 512 + np.arange(512)[None, :]
        masks[m] = (kk <= qq).astype(np.float32)
    return masks.astype(BF)


def _mk_rope():
    inv = 1.0 / (10000.0 ** (np.arange(0, 128, 2, dtype=np.float32) / 128))
    ropeR = np.zeros((128, 128), np.float32)
    for do in range(128):
        if do % 2 == 0:
            ropeR[do + 1, do] = -1.0
        else:
            ropeR[do - 1, do] = 1.0
    return inv, ropeR


def _core_sel(c):
    ja, jb_ = core_chunks(c)
    return np.r_[ja * 1024:(ja + 1) * 1024, jb_ * 1024:(jb_ + 1) * 1024]


def kernel(**inp):
    inp = {k: np.asarray(v) for k, v in inp.items()}
    x = inp["L0_attn_x"]
    masks = _mk_masks()
    idf = np.eye(128, dtype=np.float32)
    inv, ropeR = _mk_rope()

    ncA = build_launch_a()
    in_A = []
    for c in range(8):
        b = c // 4
        sel = _core_sel(c)
        in_A.append(dict(
            xo=np.ascontiguousarray(x[b, sel, :]),
            wq=inp["L0_attn_Wq"], wk=inp["L0_attn_Wk"], wv=inp["L0_attn_Wv"],
            wr=np.ascontiguousarray(inp["L0_ffn_Wr"].T),
            brt=np.ascontiguousarray(inp["L0_ffn_br"]).reshape(1, 1),
        ))
    resA = run_bass_kernel_spmd(ncA, in_A, core_ids=list(range(8)))
    xn_b = [np.empty((S, D), BF) for _ in range(2)]
    q0_b = [np.empty((D, S), BF) for _ in range(2)]
    k0_b = [np.empty((KV0, S), BF) for _ in range(2)]
    v0_b = [np.empty((S, KV0), BF) for _ in range(2)]
    lg_b = [np.empty(S, np.float32) for _ in range(2)]
    for c in range(8):
        b = c // 4
        sel = _core_sel(c)
        r = resA.results[c]
        xn_b[b][sel, :] = r["xn_tm"]
        q0_b[b][:, sel] = r["q0"]
        k0_b[b][:, sel] = r["k0"]
        v0_b[b][sel, :] = r["v0"]
        lg_b[b][sel] = r["lg"][0]

    ncB = build_launch_b()
    in_B = []
    for c in range(8):
        b = c // 4
        ja, jb_ = core_chunks(c)
        sel = _core_sel(c)
        units = unit_list(c)
        q0s = np.stack([np.ascontiguousarray(
            q0_b[b][:, (ja if ql == 0 else jb_) * 1024:((ja if ql == 0 else jb_) + 1) * 1024])
            for (ql, kc) in units])
        k0s = np.stack([np.ascontiguousarray(k0_b[b][:, kc * 1024:(kc + 1) * 1024])
                        for (ql, kc) in units])
        v0s = np.stack([np.ascontiguousarray(v0_b[b][kc * 1024:(kc + 1) * 1024, :])
                        for (ql, kc) in units])
        cmbw = np.zeros((128, 18), np.float32)
        for u, (ql, kc) in enumerate(units):
            cmbw[:, 0 * 9 + u] = 1.0 if ql == 0 else 0.0
            cmbw[:, 1 * 9 + u] = 1.0 if ql == 1 else 0.0
        pos = np.concatenate([np.arange(ja * 1024, (ja + 1) * 1024),
                              np.arange(jb_ * 1024, (jb_ + 1) * 1024)])
        ang = pos[None, :] * np.repeat(inv, 2)[:, None]
        in_B.append(dict(
            xo=np.ascontiguousarray(x[b, sel, :]),
            xn_tm=np.ascontiguousarray(xn_b[b][sel, :]),
            q0s=q0s, k0s=k0s, v0s=v0s,
            lgb=lg_b[b].reshape(1, S),
            lgo=np.ascontiguousarray(lg_b[b][sel]).reshape(1, R),
            masks=masks, cmbw=cmbw, idf=idf,
            wo0=inp["L0_attn_Wo"], bo=np.ascontiguousarray(inp["L0_attn_bo"]).reshape(1, D),
            w1=inp["L0_ffn_W1"], b1=np.ascontiguousarray(inp["L0_ffn_b1"]).reshape(1, DFF),
            w2=inp["L0_ffn_W2"], b2=np.ascontiguousarray(inp["L0_ffn_b2"]).reshape(1, D),
            wqkvT=np.ascontiguousarray(inp["L1_attn_W_qkv"].T),
            cosq=np.cos(ang).astype(np.float32), sinq=np.sin(ang).astype(np.float32),
            ropeR=ropeR,
        ))
    resB = run_bass_kernel_spmd(ncB, in_B, core_ids=list(range(8)))
    x1_b = [np.empty((S, D), np.float32) for _ in range(2)]
    xn1_b = [np.empty((S, D), BF) for _ in range(2)]
    q1_b = [np.empty((D, S), BF) for _ in range(2)]
    k1_b = [np.empty((D, S), BF) for _ in range(2)]
    v1_b = [np.empty((S, D), BF) for _ in range(2)]
    for c in range(8):
        b = c // 4
        sel = _core_sel(c)
        r = resB.results[c]
        x1_b[b][sel, :] = r["x1o"]
        xn1_b[b][sel, :] = r["xn1_tm"]
        q1_b[b][:, sel] = r["q1"]
        k1_b[b][:, sel] = r["k1"]
        v1_b[b][sel, :] = r["v1"]

    ncC = build_launch_c()
    in_C = []
    for c in range(8):
        b = c // 4
        ja, jb_ = core_chunks(c)
        sel = _core_sel(c)
        units = unit_list(c)
        q1s = np.stack([np.ascontiguousarray(
            q1_b[b][:, (ja if ql == 0 else jb_) * 1024:((ja if ql == 0 else jb_) + 1) * 1024])
            for (ql, kc) in units])
        k1s = np.stack([np.ascontiguousarray(k1_b[b][:, kc * 1024:(kc + 1) * 1024])
                        for (ql, kc) in units])
        v1s = np.stack([np.ascontiguousarray(v1_b[b][kc * 1024:(kc + 1) * 1024, :])
                        for (ql, kc) in units])
        cmbw = np.zeros((128, 18), np.float32)
        for u, (ql, kc) in enumerate(units):
            cmbw[:, 0 * 9 + u] = 1.0 if ql == 0 else 0.0
            cmbw[:, 1 * 9 + u] = 1.0 if ql == 1 else 0.0
        in_C.append(dict(
            x1i=np.ascontiguousarray(x1_b[b][sel, :]),
            xn1_tm=np.ascontiguousarray(xn1_b[b][sel, :]),
            q1s=q1s, k1s=k1s, v1s=v1s,
            masks=masks, cmbw=cmbw,
            wo1T=np.ascontiguousarray(inp["L1_attn_W_o"].T),
            wqgT=np.ascontiguousarray(inp["L1_ffn_wq_gate"].T),
            wquT=np.ascontiguousarray(inp["L1_ffn_wq_up"].T),
            wqdT=np.ascontiguousarray(inp["L1_ffn_wq_down"].T),
            sg=np.ascontiguousarray(inp["L1_ffn_s_gate"]).reshape(1, HID),
            su=np.ascontiguousarray(inp["L1_ffn_s_up"]).reshape(1, HID),
            sd=np.ascontiguousarray(inp["L1_ffn_s_down"]).reshape(1, D),
        ))
    resC = run_bass_kernel_spmd(ncC, in_C, core_ids=list(range(8)))
    out = np.empty((B, S, D), np.float32)
    for c in range(8):
        b = c // 4
        sel = _core_sel(c)
        out[b, sel, :] = resC.results[c]["out"]
    return out
